# version 05
# baseline (speedup 1.0000x reference)
import numpy as np
from contextlib import ExitStack

import concourse.bass as bass
import concourse.mybir as mybir
from concourse.bass_utils import run_bass_kernel_spmd

F32 = mybir.dt.float32
BF16 = mybir.dt.bfloat16
I32 = mybir.dt.int32
AF = mybir.ActivationFunctionType
ALU = mybir.AluOpType
AX = mybir.AxisListType

B, T, D, H, HD, DH = 4, 4096, 1024, 16, 64, 4096
G = 8
GR = 256
NOWN = G * GR
NHALO = 16
NTOK = NOWN + NHALO
NCH = D // 128
HCH = DH // 128

ENGINES = ("sync", "gpsimd", "vector", "scalar", "tensor")


class Sched:
    def __init__(self):
        self.ops = []
        self.cnt = {"pe": 0, "dv": 0, "sc": 0, "dvc": 0}

    @staticmethod
    def _flat(wait):
        out = []
        for w in wait:
            if w is None:
                continue
            if isinstance(w, list) or (isinstance(w, tuple) and w
                                       and isinstance(w[0], (tuple, list))):
                out.extend(x for x in w if x is not None)
            else:
                out.append(w)
        return tuple(out)

    def add(self, eng, fn, wait=(), inc=None, amt=1):
        wait = self._flat(wait)
        tok = None
        incs = ()
        if inc is not None:
            self.cnt[inc] = self.cnt.get(inc, 0) + amt
            tok = (inc, self.cnt[inc])
            incs = ((inc, amt),)
        self.ops.append((eng, fn, wait, incs))
        return tok

    def dma(self, eng, out, in_, wait=(), sem="misc"):
        return self.add(eng, lambda e, o=out, i=in_: e.dma_start(out=o, in_=i),
                        wait=wait, inc=sem, amt=16)

    def prepare(self):
        out = []
        totals = {}
        prev_tok = None
        for (e, fn, wait, incs) in self.ops:
            if e == "vector" and not incs:
                self.cnt["dvc"] = self.cnt.get("dvc", 0) + 1
                incs = (("dvc", 1),)
            for (sn, n) in incs:
                totals[sn] = totals.get(sn, 0) + n
            if e == "vector":
                if prev_tok is not None and prev_tok not in wait:
                    wait = wait + (prev_tok,)
                sn, n = incs[0]
                prev_tok = (sn, totals[sn])
            out.append((e, fn, wait, incs))
        self.ops = out

    def emit(self, sems, block):
        self.prepare()
        for engname in ENGINES:
            ops = [(f, w, i) for (e, f, w, i) in self.ops if e == engname]

            def make(ops=ops):
                def body(eng):
                    lw = {}
                    for fn, waits, incs in ops:
                        for (sn, v) in waits:
                            if lw.get(sn, 0) < v:
                                eng.wait_ge(sems[sn], v)
                                lw[sn] = v
                        inst = fn(eng)
                        for (sn, n) in incs:
                            if inst is None:
                                inst = eng.nop()
                            inst.then_inc(sems[sn], n)
                return body

            getattr(block, engname)(make())


class TV:

    def __init__(self, t, tcols, row0, nrows, col0, ncols):
        self.t, self.tcols = t, tcols
        self.row0, self.nrows, self.col0, self.ncols = row0, nrows, col0, ncols

    def __getitem__(self, idx):
        if not isinstance(idx, tuple):
            idx = (idx, slice(None))
        rs, cs = idx
        r0, r1, _ = rs.indices(self.nrows)
        c0, c1, _ = cs.indices(self.ncols)
        return self.t[self.row0 + r0:self.row0 + r1,
                      self.col0 + c0:self.col0 + c1]

    def ap_at(self, offset, tail, npart=128):
        return bass.AP(self.t, (self.row0 * self.tcols) + self.col0 + offset,
                       [[self.tcols, npart]] + tail)


ABF_COLS = 32768
AF32_COLS = 21568
AI32_COLS = 1024


def build_program(debug=()):
    nc = bass.Bass("TRN2", target_bir_lowering=False)
    d = {}

    def din(name, shape, dt=F32):
        d[name] = nc.dram_tensor(name, shape, dt, kind="ExternalInput")

    def dscr(name, shape, dt):
        kind = "ExternalOutput" if name in debug else "Internal"
        d[name] = nc.dram_tensor(name, shape, dt, kind=kind)

    din("xT", [D, T + NHALO])
    din("wqkv_t", [D, 3 * D])
    din("wky_t", [D, D])
    din("wproj_t", [D, D])
    din("w1_t", [D, DH])
    din("w2_t", [DH, D])
    din("convw", [DH, 3])
    din("convb", [DH, 1])
    din("identf", [128, 128], F32)
    din("identb", [128, 128], BF16)
    din("ones128", [128, 128], F32)
    din("onesp", [128, 1], F32)
    din("sel2", [66, 128], F32)
    din("mclamp", [128, 1024], BF16)
    din("m01", [128, 1024], BF16)
    din("hclamp", [128, T], BF16)
    din("h01", [128, T], BF16)
    din("hvalid", [128, NHALO], F32)
    din("ln127", [128, 1], F32)
    d["y"] = nc.dram_tensor("y", [D, NOWN], F32, kind="ExternalOutput")

    dscr("k_scr", [D, T], BF16)
    dscr("ky_scr", [D, T], BF16)
    dscr("v_scr", [NCH, 128, T], BF16)
    dscr("t1_scr", [NCH, 128, DH], BF16)
    dscr("t2_scr", [HCH, 128, D], BF16)
    dscr("h2_scr", [HCH, 128, NOWN], BF16)
    dscr("x2_scr", [D, NTOK], F32)
    if "taps" in debug:
        for nm, shp in (("dbg_hA", [128, 2048]), ("dbg_whalo", [128, 2048]),
                        ("dbg_aggv", [128, 32]), ("dbg_rswT", [128, 288]),
                        ("dbg_outn", [128, 272]), ("dbg_bcast", [128, 272])):
            d[nm] = nc.dram_tensor(nm, shp, F32 if nm != "dbg_whalo" else BF16,
                                   kind="ExternalOutput")
    if "taps" in debug:
        d["dbg_q"] = nc.dram_tensor("dbg_q", [128, 64], BF16, kind="ExternalOutput")
        d["dbg_st"] = nc.dram_tensor("dbg_st", [2, 512], F32, kind="ExternalOutput")
    if "ttaps" in debug:
        d["dbg_tern"] = nc.dram_tensor("dbg_tern", [8, 128], F32,
                                       kind="ExternalOutput")
    if "ctaps" in debug:
        for nm, shp, dt_ in (("dbg_kx", [128, NTOK], BF16),
                             ("dbg_sigb", [128, 2080], F32),
                             ("dbg_hsb", [128, 2080], F32),
                             ("dbg_m2", [128, 2048], F32),
                             ("dbg_s2", [128, 2048], F32)):
            d[nm] = nc.dram_tensor(nm, shp, dt_, kind="ExternalOutput")
    st_extra = {"taps": "taps" in debug, "ctaps": "ctaps" in debug,
                "ttaps": "ttaps" in debug}

    ctx = ExitStack()
    base = {}

    def sb(name, shape, dt=F32):
        base[name] = ctx.enter_context(nc.sbuf_tensor("s_" + name, shape, dt))

    with ctx:
        sb("identf", [128, 128])
        sb("identb", [128, 128], BF16)
        sb("ones128", [128, 128])
        sb("onesp", [128, 1])
        sb("sel2", [66, 128])
        sb("mclamp", [128, 1024], BF16)
        sb("m01", [128, 1024], BF16)
        sb("hvalid", [128, NHALO])
        sb("ln127", [128, 1])
        sb("convw", [128, 3 * HCH])
        sb("convb", [128, HCH])
        sb("wabs", [128, 33])
        sb("sw1col", [128, 1])
        sb("sw2col", [128, 1])
        sb("dumw", [128, 1])
        sb("kev0", [128, 512], BF16)
        sb("kev1", [128, 512], BF16)
        sb("vtmp", [128, 512], BF16)
        sb("vout", [128, 512], BF16)
        for c in range(NCH):
            sb(f"Q{c}", [128, NTOK], BF16)
        for c in range(NCH):
            sb(f"xln{c}", [128, 512], BF16)
        sb("abf", [128, ABF_COLS], BF16)
        sb("af32", [128, AF32_COLS], F32)
        sb("ai32", [128, AI32_COLS], I32)

        abf, af32, ai32 = base["abf"], base["af32"], base["ai32"]

        def vbf(col0, ncols, row0=0, nrows=128):
            return TV(abf, ABF_COLS, row0, nrows, col0, ncols)

        def vf(col0, ncols, row0=0, nrows=128):
            return TV(af32, AF32_COLS, row0, nrows, col0, ncols)

        def vi(col0, ncols):
            return TV(ai32, AI32_COLS, 0, 128, col0, ncols)

        sA = dict(base)
        for c in range(NCH):
            sA[f"wqkv{c}"] = vbf(3072 * c, 3072)
            sA[f"wky{c}"] = vbf(24576 + 1024 * c, 1024)
        sA["wf32"] = vf(0, 4096)
        for c in range(NCH):
            sA[f"xa{c}"] = vf(4096 + 512 * c, 512)
        sA["sq"] = vf(8192, 512)
        sA["lns"] = vf(8704, 512)
        sA["mb"] = vf(9216, 512)
        sA["rb"] = vf(9728, 512)
        sA["terni"] = vi(0, 1024)
        sA["ternb"] = vbf(0, 4096)

        sB = dict(base)
        sB["lyA"] = vf(0, 4096)
        sB["lyB"] = vf(4096, 4096)
        sB["hA"] = vf(8192, 2048)
        sB["bcast"] = vf(10240, 272)
        sB["resbuf"] = vf(10512, 272)
        sB["x2st"] = vf(10784, 272)
        sB["aggv"] = vf(11056, 32)
        sB["rswT"] = vf(11088, 288)
        sB["wbA"] = vbf(0, 4096)
        sB["wbB"] = vbf(4096, 4096)
        sB["whalo"] = vbf(8192, 2048)
        sB["kyb0"] = vbf(10240, 512)
        sB["kyb1"] = vbf(10752, 512)
        sB["kbf0"] = vbf(11264, 512)
        sB["kbf1"] = vbf(11776, 512)
        sB["vb0"] = vbf(12288, 512)
        sB["vb1"] = vbf(12800, 512)
        sB["wt0"] = vbf(13312, 256)
        sB["wt1"] = vbf(13568, 256)
        sB["wth"] = vbf(13824, 128)
        for hp in range(NCH):
            sB[f"outn{hp}"] = vbf(13952 + 272 * hp, 272)
        sB["hclamp"] = vbf(16384, 4096)
        sB["h01"] = vbf(20480, 4096)
        for c in range(NCH):
            sB[f"wproj{c}"] = vbf(24576 + 1024 * c, 1024)
        sB["wf32"] = vf(0, 4096)

        sC = dict(base)
        for c in range(NCH):
            sC[f"xa{c}"] = vf(512 * c, 512)
            sC[f"x2ln{c}"] = vf(4096 + 512 * c, 512)
        sC["s2row"] = vf(8192, 2048)
        sC["lns"] = vf(10240, 512)
        sC["mb"] = vf(10752, 512)
        sC["rb"] = vf(11264, 512)
        sC["sb_"] = vf(11776, 512)
        sC["mx"] = vf(12288, 512)
        sC["mxt"] = vf(12800, 512)
        sC["sg2b"] = vf(13312, 512)
        sC["ostage0"] = vf(4096, 512)
        sC["ostage1"] = vf(4608, 512)
        sC["sigb"] = vf(13824, 2080)
        sC["m2glob"] = vf(15904, 2048)
        sC["hsb"] = vf(17952, 2080)
        sC["zrow"] = vf(20032, 512)
        sC["hacc"] = vf(0, 2048)
        sC["hacc2"] = vf(4096, 2048)
        sC["h2outv"] = vbf(3072, 2048)
        sC["t1t"] = vbf(0, 1024)
        sC["t2t0"] = vbf(1024, 1024)
        sC["t2t1"] = vbf(2048, 1024)
        sC["h2in0"] = vbf(5120, 512)
        sC["h2in1"] = vbf(5632, 512)
        sC["k2b"] = vbf(6144, 512)
        sC["i32s"] = vi(0, 512)

        p = {}
        with ExitStack() as pctx:
            for nm in ("pA0", "pA1", "pB0", "pB1", "pav", "pm"):
                p[nm] = pctx.enter_context(nc.psum_tensor(nm, [128, 512], F32))
            with ExitStack() as pwctx:
                p["pw0"] = pwctx.enter_context(nc.psum_tensor("pw0", [128, 512], BF16))
                p["pw1"] = pwctx.enter_context(nc.psum_tensor("pw1", [128, 512], BF16))
                S = Sched()
                st = dict(st_extra)
                _startup(S, d, sA, st)
                _phase_a(S, d, sA, p, st)
                _tern(S, d, sA, p, st)
                _wproj_load(S, d, sB, st)
                _phase_b(S, d, sB, p, st)
            p["pc6"] = pctx.enter_context(nc.psum_tensor("pc6", [128, 512], F32))
            p["pc7"] = pctx.enter_context(nc.psum_tensor("pc7", [128, 512], F32))
            _phase_c(S, d, sC, p, st)

            sems = {}
            with ExitStack() as sctx:
                for nm in S.cnt:
                    sems[nm] = sctx.enter_context(nc.semaphore(nm))
                with nc.Block() as block:
                    S.emit(sems, block)
    return nc


def _ap(t):
    return t.ap()


def _startup(S, d, s, st):
    ctok = None
    for name in ("identf", "identb", "ones128", "onesp", "sel2",
                 "mclamp", "m01", "hvalid", "ln127"):
        ctok = S.dma("sync", s[name][:], _ap(d[name]), sem="cst")
    for hc in range(HCH):
        S.dma("sync", s["convw"][:, 3 * hc:3 * hc + 3],
              _ap(d["convw"])[128 * hc:128 * (hc + 1), :], sem="cst")
        ctok = S.dma("sync", s["convb"][:, hc:hc + 1],
                     _ap(d["convb"])[128 * hc:128 * (hc + 1), :], sem="cst")
    for eng in ("gpsimd", "vector", "scalar", "tensor"):
        S.add(eng, lambda e: e.nop(), wait=(ctok,))
    wf_rel = None
    for c in range(NCH):
        for (nm, src, w) in (("wqkv", "wqkv_t", 3 * D), ("wky", "wky_t", D)):
            t_l = S.dma("sync", s["wf32"][:, 0:w],
                        _ap(d[src])[128 * c:128 * (c + 1), :], wait=(wf_rel,),
                        sem="wf")
            wf_rel = S.add("vector", lambda e, c=c, nm=nm, w=w: e.tensor_copy(
                out=s[f"{nm}{c}"][:, 0:w], in_=s["wf32"][:, 0:w]),
                wait=(t_l,), inc="dv")
    st["wq_ready"] = wf_rel
    st["wf_rel"] = wf_rel


def _ln_block(S, s, p, st, ch_tok, width, stats_rel, sq_name="sq"):
    sq_rel = st.get("_sq_rel")
    t_sx = t_sq = None
    for c in range(NCH):
        t_sx = S.add("tensor", lambda e, c=c, width=width: e.matmul(
            p["pB0"][:, 0:width], s["ones128"][:], s[f"xa{c}"][:, 0:width],
            start=(c == 0), stop=(c == NCH - 1)),
            wait=(ch_tok[c], stats_rel[0] if c == 0 else None), inc="pe")
        t_q = S.add("scalar", lambda e, c=c, width=width, sq_name=sq_name:
                    e.activation(
            out=s[sq_name][:, 0:width], in_=s[f"xa{c}"][:, 0:width],
            func=AF.Square), wait=(ch_tok[c], sq_rel), inc="sc")
        t_sq = S.add("tensor", lambda e, c=c, width=width, sq_name=sq_name: e.matmul(
            p["pB1"][:, 0:width], s["ones128"][:], s[sq_name][:, 0:width],
            start=(c == 0), stop=(c == NCH - 1)),
            wait=(t_q, stats_rel[1] if c == 0 else None), inc="pe")
        sq_rel = t_sq
    st["_sq_rel"] = t_sq
    t_m = S.add("vector", lambda e, width=width: e.tensor_scalar(
        out=s["mb"][:, 0:width], in0=p["pB0"][:, 0:width],
        scalar1=1.0 / D, scalar2=None, op0=ALU.mult), wait=(t_sx,), inc="dv")
    stats_rel[0] = t_m
    t_q2 = S.add("vector", lambda e, width=width: e.tensor_scalar(
        out=s["rb"][:, 0:width], in0=p["pB1"][:, 0:width],
        scalar1=1.0 / D, scalar2=None, op0=ALU.mult), wait=(t_sq,), inc="dv")
    stats_rel[1] = t_q2
    S.add("vector", lambda e, width=width, sq_name=sq_name: e.tensor_mul(
        s[sq_name][:, 0:width], s["mb"][:, 0:width], s["mb"][:, 0:width]))
    S.add("vector", lambda e, width=width, sq_name=sq_name: e.tensor_sub(
        s["rb"][:, 0:width], s["rb"][:, 0:width], s[sq_name][:, 0:width]))
    S.add("vector", lambda e, width=width: e.tensor_scalar(
        out=s["rb"][:, 0:width], in0=s["rb"][:, 0:width],
        scalar1=1e-5, scalar2=None, op0=ALU.add))
    t_va = S.add("vector", lambda e: e.nop(), inc="dv")
    S.add("scalar", lambda e, width=width: e.activation(
        out=s["lns"][:, 0:width], in_=s["rb"][:, 0:width], func=AF.Ln),
        wait=(t_va,))
    t_re = S.add("scalar", lambda e, width=width: e.activation(
        out=s["rb"][:, 0:width], in_=s["lns"][:, 0:width], func=AF.Exp,
        scale=-0.5), inc="sc")
    return t_re


def _phase_a(S, d, s, p, st):
    xa_rel = [None] * NCH
    xln_rel = None
    kev_rel = [None, None]
    pa_rel = [None, None]
    stats_rel = [None, None]
    pm_rel = None
    pw_rel = [None, None]
    q_last = None
    kv_fin = {}
    wq = st["wq_ready"]
    last_proj = None

    for tg in range(9):
        width = 512 if tg < 8 else NHALO
        col0 = 512 * tg
        ch_tok = []
        for c in range(NCH):
            t_l = S.dma("sync", s[f"xa{c}"][:, 0:width],
                        _ap(d["xT"])[128 * c:128 * (c + 1), col0:col0 + width],
                        wait=(xa_rel[c], wq if tg == 0 else None), sem=f"xa{c}")
            ch_tok.append(t_l)
        t_re = _ln_block(S, s, p, st, ch_tok, width, stats_rel)
        xln_tok = []
        for c in range(NCH):
            S.add("vector", lambda e, c=c, width=width: e.tensor_sub(
                s["sq"][:, 0:width], s[f"xa{c}"][:, 0:width], s["mb"][:, 0:width]),
                wait=(t_re, ch_tok[c], st["_sq_rel"] if c == 0 else None))
            t_x = S.add("vector", lambda e, c=c, width=width: e.tensor_mul(
                s[f"xln{c}"][:, 0:width], s["sq"][:, 0:width], s["rb"][:, 0:width]),
                wait=(xln_rel,) if c == 0 else (), inc="dv")
            xln_tok.append(t_x)
            xa_rel[c] = t_x
        st["_sq_rel"] = xln_tok[-1]
        fams = []
        if tg < 8:
            fams += [("K", "wqkv", D, "k_scr"), ("Ky", "wky", 0, "ky_scr"),
                     ("V", "wqkv", 2 * D, "v_scr")]
        fams += [("Q", "wqkv", 0, None)]
        for fam, wnm, bofs, scr in fams:
            pw_ = 256 if (fam == "Q" and tg < 8) else width
            for f in range(NCH):
                alt = f % 2
                bank = "pA0" if alt == 0 else "pA1"
                t_mm = None
                for dc in range(NCH):
                    t_mm = S.add("tensor", lambda e, wnm=wnm, bofs=bofs, f=f, dc=dc,
                                 bank=bank, pw_=pw_: e.matmul(
                        p[bank][:, 0:pw_],
                        s[f"{wnm}{dc}"][:, bofs + 128 * f: bofs + 128 * (f + 1)],
                        s[f"xln{dc}"][:, 0:pw_],
                        start=(dc == 0), stop=(dc == NCH - 1)),
                        wait=(xln_tok[dc],)
                        + ((pa_rel[alt], wq) if dc == 0 else ()),
                        inc="pe" if dc == NCH - 1 else None)
                last_proj = t_mm
                if fam == "Q":
                    qc0 = 256 * tg if tg < 8 else NOWN
                    t_e = S.add("scalar", lambda e, f=f, bank=bank, qc0=qc0, pw_=pw_:
                                e.activation(
                        out=s[f"Q{f}"][:, qc0:qc0 + pw_], in_=p[bank][:, 0:pw_],
                        func=AF.Copy, scale=0.125), wait=(t_mm,), inc="sc")
                    pa_rel[alt] = t_e
                    q_last = t_e
                elif fam in ("K", "Ky"):
                    kv = f"kev{alt}"
                    t_e = S.add("scalar", lambda e, kv=kv, bank=bank, pw_=pw_:
                                e.activation(
                        out=s[kv][:, 0:pw_], in_=p[bank][:, 0:pw_], func=AF.Copy),
                        wait=(t_mm, kev_rel[alt]), inc="sc")
                    pa_rel[alt] = t_e
                    t_w = S.dma("gpsimd",
                                _ap(d[scr])[128 * f:128 * (f + 1), col0:col0 + pw_],
                                s[kv][:, 0:pw_], wait=(t_e,), sem=f"ke{alt}")
                    kev_rel[alt] = t_w
                    kv_fin[f"ke{alt}"] = t_w
                else:
                    t_e = S.add("scalar", lambda e, bank=bank, pw_=pw_: e.activation(
                        out=s["vtmp"][:, 0:pw_], in_=p[bank][:, 0:pw_], func=AF.Copy),
                        wait=(t_mm, pw_rel[1]), inc="sc")
                    pa_rel[alt] = t_e
                    ev = []
                    for i in range(4):
                        pw = f"pw{i // 2}"
                        t_t = S.add("tensor", lambda e, i=i, pw=pw: e.transpose(
                            p[pw][:, 128 * (i % 2):128 * (i % 2 + 1)],
                            s["vtmp"][:, 128 * i:128 * (i + 1)], s["identb"][:]),
                            wait=(t_e, pw_rel[i // 2] if i % 2 == 0 else None),
                            inc="pe" if i % 2 == 1 else None)
                        if i % 2 == 1:
                            t_v = S.add("vector", lambda e, i=i, pw=pw: e.tensor_copy(
                                out=s["vout"][:, 128 * (i - 1):128 * (i + 1)],
                                in_=p[pw][:, 0:256]), wait=(t_t,), inc="dv")
                            pw_rel[i // 2] = t_v
                            ev.append(t_v)
                    t_w = S.dma("gpsimd", _ap(d["v_scr"])[f][:, col0:col0 + 512],
                                s["vout"][:], wait=tuple(ev), sem="vo")
                    pw_rel[1] = t_w
                    kv_fin["vo"] = t_w
        xln_rel = last_proj

    st["q_done"] = q_last
    st["kv_done"] = list(kv_fin.values())
    st["statsB"] = tuple(stats_rel)
    st["pwA"] = tuple(pw_rel)
    st["pmA"] = pm_rel
    st["projA"] = last_proj
    st["paA"] = tuple(pa_rel)
    if st.get("taps"):
        S.dma("gpsimd", _ap(d["dbg_st"])[0:1, :], s["mb"][0:1, 0:512],
              wait=(q_last,), sem="dbg")
        S.dma("gpsimd", _ap(d["dbg_st"])[1:2, :], s["rb"][0:1, 0:512],
              wait=(q_last,), sem="dbg")


def _tern(S, d, s, p, st):
    wf_rel = st["wf_rel"]
    pm_rel = st["pmA"]
    abf_gate = st["projA"]
    ternb_rel = None
    toks = []
    for (wname, scrname, nrow, W, scol) in (
            ("w1_t", "t1_scr", NCH, DH, "sw1col"),
            ("w2_t", "t2_scr", HCH, D, "sw2col")):
        t_q = None
        for c in range(nrow):
            t_l = S.dma("sync", s["wf32"][:, 0:W],
                        _ap(d[wname])[128 * c:128 * (c + 1), :], wait=(wf_rel,),
                        sem="wf")
            t_q = S.add("scalar", lambda e, c=c, W=W: e.activation(
                out=s["ternb"][:, 0:W], in_=s["wf32"][:, 0:W], func=AF.Abs,
                accum_out=s["wabs"][:, c:c + 1]),
                wait=(t_l, abf_gate if (c == 0 and wname == "w1_t") else None,
                      ternb_rel), inc="sc")
            ternb_rel = t_q
            wf_rel = t_q
        acc_tok = S.add("vector", lambda e, nrow=nrow, scol=scol: e.tensor_reduce(
            out=s[scol][:], in_=s["wabs"][:, 0:nrow], axis=AX.X, op=ALU.add),
            wait=(t_q,), inc="dv")
        tcol = 0 if wname == "w1_t" else 3
        d_tok = None
        if st.get("ttaps"):
            d_tok = S.dma("gpsimd", _ap(d["dbg_tern"])[tcol],
                          s[scol][:], wait=(acc_tok,), sem="dbg")
        t_mmw = S.add("tensor", lambda e, scol=scol: e.matmul(
            p["pm"][:, 0:1], s["ones128"][:], s[scol][:], start=True, stop=True),
            wait=(acc_tok, pm_rel), inc="pe")
        n_el = float(nrow * 128 * W)
        t_c1 = S.add("vector", lambda e, scol=scol, n_el=n_el: e.tensor_scalar(
            out=s[scol][:], in0=p["pm"][:, 0:1], scalar1=1.0 / n_el, scalar2=1e-5,
            op0=ALU.mult, op1=ALU.max), wait=(t_mmw, d_tok), inc="dv")
        pm_rel = t_c1
        t_sv = S.add("vector", lambda e, scol=scol: e.tensor_copy(
            out=s["wabs"][:, 32:33], in_=s[scol][:]), wait=(t_c1,), inc="dv")
        d_tok2 = None
        if st.get("ttaps"):
            d_tok2 = S.dma("gpsimd", _ap(d["dbg_tern"])[tcol + 1],
                           s[scol][:], wait=(t_c1,), sem="dbg")
        S.add("scalar", lambda e, scol=scol: e.activation(
            out=s["dumw"][:], in_=s[scol][:], func=AF.Ln), wait=(t_c1,))
        t_s2 = S.add("scalar", lambda e, scol=scol: e.activation(
            out=s[scol][:], in_=s["dumw"][:], func=AF.Exp, scale=-1.0),
            wait=(d_tok2, t_sv), inc="sc")
        t_n1 = S.add("vector", lambda e, scol=scol: e.tensor_mul(
            s["dumw"][:], s[scol][:], s["wabs"][:, 32:33]), wait=(t_s2,),
            inc="dv")
        t_n2 = S.add("vector", lambda e: e.tensor_scalar(
            out=s["dumw"][:], in0=s["dumw"][:], scalar1=-1.0, scalar2=2.0,
            op0=ALU.mult, op1=ALU.add), wait=(t_n1,), inc="dv")
        t_s2 = S.add("vector", lambda e, scol=scol: e.tensor_mul(
            s[scol][:], s[scol][:], s["dumw"][:]), wait=(t_n2,), inc="dv")
        if st.get("ttaps"):
            S.dma("gpsimd", _ap(d["dbg_tern"])[tcol + 2],
                  s[scol][:], wait=(t_s2,), sem="dbg")
        for c in range(nrow):
            t_l = S.dma("sync", s["wf32"][:, 0:W],
                        _ap(d[wname])[128 * c:128 * (c + 1), :], wait=(wf_rel,),
                        sem="wf")
            S.add("vector", lambda e, W=W, scol=scol: e.tensor_scalar(
                out=s["wf32"][:, 0:W], in0=s["wf32"][:, 0:W], scalar1=s[scol][:],
                scalar2=-1.49, op0=ALU.mult, op1=ALU.max), wait=(t_l, t_s2))
            nh = (W + 1023) // 1024
            t_3 = None
            for hh in range(nh):
                hw = min(1024, W - 1024 * hh)
                S.add("vector", lambda e, hh=hh, hw=hw: e.tensor_scalar(
                    out=s["terni"][:, 0:hw],
                    in0=s["wf32"][:, 1024 * hh:1024 * hh + hw],
                    scalar1=1.49, scalar2=None, op0=ALU.min),
                    wait=(t_3,) if t_3 else ())
                gate = (abf_gate,) if (hh == 0 and c == 0 and wname == "w1_t") else ()
                if hh == 0:
                    gate = gate + (ternb_rel,)
                t_3 = S.add("vector", lambda e, hh=hh, hw=hw: e.tensor_copy(
                    out=s["ternb"][:, 1024 * hh:1024 * hh + hw],
                    in_=s["terni"][:, 0:hw]), wait=gate, inc="dv")
            t_w = S.dma("gpsimd", _ap(d[scrname])[c], s["ternb"][:, 0:W], wait=(t_3,),
                        sem="tw")
            ternb_rel = t_w
            toks.append(t_w)
            wf_rel = t_3
    st["tern_done"] = toks[-1]
    st["wf_rel"] = wf_rel
    st["pm_tern"] = pm_rel


def _wproj_load(S, d, s, st):
    wf_rel = st["wf_rel"]
    last = None
    for c in range(NCH):
        t_l = S.dma("sync", s["wf32"][:, 0:D],
                    _ap(d["wproj_t"])[128 * c:128 * (c + 1), :], wait=(wf_rel,),
                    sem="wf")
        last = S.add("vector", lambda e, c=c: e.tensor_copy(
            out=s[f"wproj{c}"][:, 0:D], in_=s["wf32"][:, 0:D]),
            wait=(t_l, st["tern_done"] if c == 0 else None), inc="dv")
        wf_rel = last
    st["wproj_ready"] = last
    st["wf_final"] = wf_rel
    if st.get("taps"):
        S.dma("gpsimd", _ap(d["dbg_q"]), s["Q0"][:, 2000:2064],
              wait=(st["q_done"],), sem="dbg")
    t1 = S.dma("sync", s["hclamp"][:], _ap(d["hclamp"]), wait=(st["projA"],),
               sem="cst")
    t2 = S.dma("sync", s["h01"][:], _ap(d["h01"]), sem="hm")
    st["hmask_ready"] = (t1, t2)


def _phase_b(S, d, s, p, st):
    qtok = st["q_done"]
    kvtok = st["kv_done"]
    bank_rel = {nm: None for nm in ("pA0", "pA1", "pB0", "pB1", "pm", "pav")}
    bank_rel["pA0"] = bank_rel["pA1"] = st["paA"][0]
    bank_rel["pB0"] = st["statsB"][0]
    bank_rel["pB1"] = st["statsB"][1]
    bank_rel["pm"] = st["pm_tern"]
    kyb_rel = [st["wf_final"], st["wf_final"]]
    kbf_rel = [None, None]
    vb_rel = [None, None]
    wb_rel = {0: st["tern_done"], 1: st["tern_done"]}
    wh_rel = st["hmask_ready"][1]
    wt_rel = [None, None]
    wth_rel = None
    pw_rel = list(st["pwA"])
    resbuf_rel = None
    x2st_rel = None
    outn_rel = [st["wf_final"]] * NCH
    x2_last = None

    bankof = {(0, 0): "pA0", (1, 0): "pB0", (0, 1): "pA1", (1, 1): "pB1"}
    S.add("vector", lambda e: e.memset(s["rswT"][:], 0.0))
    bank_rel["pm"] = S.add("vector", lambda e: e.memset(p["pm"][:], 0.0),
                           wait=(bank_rel["pm"],), inc="dv")

    for g in range(G):
        nchunks = g + 1
        kblocks = [(0, min(4, nchunks))]
        if nchunks > 4:
            kblocks.append((4, nchunks - 4))
        halo = (g == G - 1)
        for hp in range(NCH):
            qc0 = 256 * g
            clip_tok = {}
            exp_tok = {}
            halo_clip = None
            hexp_tok = None
            aggv_sc = {}
            for kb_i, (j0, nj) in enumerate(kblocks):
                for jj in range(nj):
                    j = j0 + jj
                    alt = j % 2
                    t_ky = S.dma("sync", s[f"kyb{alt}"][:],
                                 _ap(d["ky_scr"])[128 * hp:128 * (hp + 1),
                                                  512 * j:512 * (j + 1)],
                                 wait=(kyb_rel[alt], kvtok), sem=f"ky{alt}")
                    mm = {}
                    for t in (0, 1):
                        for h in (0, 1):
                            bank = bankof[(h, t)]
                            mm[(h, t)] = S.add("tensor", lambda e, h=h, t=t, hp=hp,
                                               qc0=qc0, alt=alt, bank=bank: e.matmul(
                                p[bank][:],
                                s[f"Q{hp}"][64 * h:64 * (h + 1),
                                            qc0 + 128 * t: qc0 + 128 * (t + 1)],
                                s[f"kyb{alt}"][64 * h:64 * (h + 1), :],
                                start=True, stop=True, tile_position=(64 * h, 0)),
                                wait=(t_ky, qtok, bank_rel[bank]), inc="pe")
                    kyb_rel[alt] = mm[(1, 1)]
                    for t in (0, 1):
                        for h in (0, 1):
                            bank = bankof[(h, t)]
                            lyb = "lyA" if h == 0 else "lyB"
                            dst_c = 2048 * t + 512 * jj
                            w_extra = (exp_tok.get((h, t))
                                       if (jj == 0 and kb_i > 0) else None)
                            if j < g:
                                t_c = S.add("vector", lambda e, bank=bank, lyb=lyb,
                                            dst_c=dst_c: e.tensor_scalar(
                                    out=s[lyb][:, dst_c:dst_c + 512], in0=p[bank][:],
                                    scalar1=-15.0, scalar2=15.0,
                                    op0=ALU.max, op1=ALU.min),
                                    wait=(mm[(h, t)], w_extra), inc="dv")
                            else:
                                t_c = S.add("vector", lambda e, bank=bank, lyb=lyb,
                                            dst_c=dst_c, t=t: e.scalar_tensor_tensor(
                                    out=s[lyb][:, dst_c:dst_c + 512], in0=p[bank][:],
                                    scalar=-15.0,
                                    in1=s["mclamp"][:, 512 * t:512 * (t + 1)],
                                    op0=ALU.max, op1=ALU.min),
                                    wait=(mm[(h, t)], w_extra), inc="dv")
                            clip_tok[(h, t)] = t_c
                            bank_rel[bank] = t_c
                    if halo:
                        S.add("tensor", lambda e, hp=hp, alt=alt: e.matmul(
                            p["pm"][0:16, :], s[f"Q{hp}"][0:64, NOWN:NTOK],
                            s[f"kyb{alt}"][0:64, :], start=True, stop=True),
                            wait=(t_ky, bank_rel["pm"], st["hmask_ready"][0]))
                        t_h1 = S.add("tensor", lambda e, hp=hp, alt=alt: e.matmul(
                            p["pm"][64:80, :], s[f"Q{hp}"][64:128, NOWN:NTOK],
                            s[f"kyb{alt}"][64:128, :], start=True, stop=True,
                            tile_position=(64, 64), skip_group_check=True), inc="pe")
                        kyb_rel[alt] = t_h1
                        wh_extra = hexp_tok if (jj == 0 and kb_i > 0) else None
                        t_hc = S.add("vector", lambda e, j=j, jj=jj:
                                     e.scalar_tensor_tensor(
                            out=s["hA"][:, 512 * jj:512 * (jj + 1)], in0=p["pm"][:],
                            scalar=-15.0, in1=s["hclamp"][:, 512 * j:512 * (j + 1)],
                            op0=ALU.max, op1=ALU.min),
                            wait=(t_h1, wh_extra), inc="dv")
                        halo_clip = t_hc
                        bank_rel["pm"] = t_hc
                for t in (0, 1):
                    for h in (0, 1):
                        lyb = "lyA" if h == 0 else "lyB"
                        col = 4 * kb_i + 2 * t + h
                        t_e = S.add("scalar", lambda e, lyb=lyb, t=t, nj=nj, col=col:
                                    e.activation(
                            out=s[lyb][:, 2048 * t:2048 * t + 512 * nj],
                            in_=s[lyb][:, 2048 * t:2048 * t + 512 * nj],
                            func=AF.Exp, accum_out=s["aggv"][:, col:col + 1]),
                            wait=(clip_tok[(h, t)],), inc="sc")
                        exp_tok[(h, t)] = t_e
                        aggv_sc[col] = t_e
                if halo:
                    col = 24 + kb_i
                    hexp_tok = S.add("scalar", lambda e, nj=nj, col=col: e.activation(
                        out=s["hA"][:, 0:512 * nj], in_=s["hA"][:, 0:512 * nj],
                        func=AF.Exp, accum_out=s["aggv"][:, col:col + 1]),
                        wait=(halo_clip,), inc="sc")
                    aggv_sc[col] = hexp_tok
            w_agg = tuple(aggv_sc[c] for c in range(4 * len(kblocks)))
            if len(kblocks) == 2:
                S.add("vector", lambda e: e.scalar_tensor_tensor(
                    out=s["aggv"][:, 8:12], in0=s["aggv"][:, 0:4], scalar=2e-6,
                    in1=s["aggv"][:, 4:8], op0=ALU.add, op1=ALU.add), wait=w_agg)
            else:
                S.add("vector", lambda e: e.tensor_scalar(
                    out=s["aggv"][:, 8:12], in0=s["aggv"][:, 0:4], scalar1=2e-6,
                    scalar2=None, op0=ALU.add), wait=w_agg)
            t_a2 = S.add("vector", lambda e: e.nop(), inc="dv")
            S.add("scalar", lambda e: e.activation(
                out=s["aggv"][:, 12:16], in_=s["aggv"][:, 8:12], func=AF.Ln),
                wait=(t_a2,))
            t_rr = S.add("scalar", lambda e: e.activation(
                out=s["aggv"][:, 12:16], in_=s["aggv"][:, 12:16], func=AF.Exp,
                scale=-1.0), inc="sc")
            S.add("vector", lambda e: e.tensor_scalar(
                out=s["aggv"][:, 16:20], in0=s["aggv"][:, 12:16], scalar1=1e-6,
                scalar2=1.0, op0=ALU.mult, op1=ALU.add), wait=(t_rr,))
            t_ag = S.add("vector", lambda e: e.tensor_scalar(
                out=s["aggv"][:, 20:24], in0=s["aggv"][:, 12:16], scalar1=1e-6,
                scalar2=None, op0=ALU.mult), wait=(t_rr,), inc="dv")
            if halo:
                S.add("vector", lambda e: e.scalar_tensor_tensor(
                    out=s["aggv"][:, 26:27], in0=s["aggv"][:, 24:25], scalar=2e-6,
                    in1=s["aggv"][:, 25:26], op0=ALU.add, op1=ALU.add),
                    wait=(aggv_sc[24], aggv_sc[25]))
                t_a3 = S.add("vector", lambda e: e.nop(), inc="dv")
                S.add("scalar", lambda e: e.activation(
                    out=s["aggv"][:, 27:28], in_=s["aggv"][:, 26:27], func=AF.Ln),
                    wait=(t_a3,))
                t_rr2 = S.add("scalar", lambda e: e.activation(
                    out=s["aggv"][:, 27:28], in_=s["aggv"][:, 27:28], func=AF.Exp,
                    scale=-1.0), inc="sc")
                t_ag = S.add("vector", lambda e: e.tensor_scalar(
                    out=s["aggv"][:, 28:29], in0=s["aggv"][:, 27:28], scalar1=1e-6,
                    scalar2=None, op0=ALU.mult), wait=(t_rr2,), inc="dv")
            av_started = False
            sw_cols = []
            abs_tok = {}
            habs_tok = None
            ln_tok = {}
            hln_tok = None
            for kb_i, (j0, nj) in enumerate(kblocks):
                for jj in range(nj):
                    j = j0 + jj
                    alt = j % 2
                    t_k = S.dma("sync", s[f"kbf{alt}"][:],
                                _ap(d["k_scr"])[128 * hp:128 * (hp + 1),
                                                512 * j:512 * (j + 1)],
                                wait=(kbf_rel[alt], kvtok), sem=f"kb{alt}")
                    mm = {}
                    for t in (0, 1):
                        for h in (0, 1):
                            bank = bankof[(h, t)]
                            mm[(h, t)] = S.add("tensor", lambda e, h=h, t=t, hp=hp,
                                               qc0=qc0, alt=alt, bank=bank: e.matmul(
                                p[bank][:],
                                s[f"Q{hp}"][64 * h:64 * (h + 1),
                                            qc0 + 128 * t: qc0 + 128 * (t + 1)],
                                s[f"kbf{alt}"][64 * h:64 * (h + 1), :],
                                start=True, stop=True, tile_position=(64 * h, 0)),
                                wait=(t_k, bank_rel[bank]), inc="pe")
                    kbf_rel[alt] = mm[(1, 1)]
                    for t in (0, 1):
                        for h in (0, 1):
                            bank = bankof[(h, t)]
                            lyb = "lyA" if h == 0 else "lyB"
                            dst_c = 2048 * t + 512 * jj
                            rcol = 12 + 2 * t + h
                            w_extra = None
                            if jj == 0:
                                w_extra = (ln_tok.get((h, t)) if kb_i > 0
                                           else exp_tok[(h, t)])
                            t_ab = S.add("scalar", lambda e, bank=bank, lyb=lyb,
                                         dst_c=dst_c, rcol=rcol: e.activation(
                                out=s[lyb][:, dst_c:dst_c + 512], in_=p[bank][:],
                                func=AF.Abs, scale=s["aggv"][:, rcol:rcol + 1]),
                                wait=(mm[(h, t)], t_ag, w_extra), inc="sc")
                            abs_tok[(h, t)] = t_ab
                            bank_rel[bank] = t_ab
                    if halo:
                        S.add("tensor", lambda e, hp=hp, alt=alt: e.matmul(
                            p["pm"][0:16, :], s[f"Q{hp}"][0:64, NOWN:NTOK],
                            s[f"kbf{alt}"][0:64, :], start=True, stop=True),
                            wait=(t_k, bank_rel["pm"]))
                        t_h1 = S.add("tensor", lambda e, hp=hp, alt=alt: e.matmul(
                            p["pm"][64:80, :], s[f"Q{hp}"][64:128, NOWN:NTOK],
                            s[f"kbf{alt}"][64:128, :], start=True, stop=True,
                            tile_position=(64, 64), skip_group_check=True), inc="pe")
                        kbf_rel[alt] = t_h1
                        wh_extra = None
                        if jj == 0:
                            wh_extra = hln_tok if kb_i > 0 else hexp_tok
                        t_hab = S.add("scalar", lambda e, jj=jj: e.activation(
                            out=s["hA"][:, 512 * jj:512 * (jj + 1)], in_=p["pm"][:],
                            func=AF.Abs, scale=s["aggv"][:, 27:28]),
                            wait=(t_h1, t_ag, wh_extra), inc="sc")
                        habs_tok = t_hab
                        bank_rel["pm"] = t_hab
                is_last_kb = (kb_i == len(kblocks) - 1)
                if is_last_kb:
                    sw_cols.append(8)
                nmain = nj - (1 if is_last_kb else 0)
                if nmain > 0:
                    sw_cols.append(4 * kb_i)
                for t in (0, 1):
                    for h in (0, 1):
                        lyb = "lyA" if h == 0 else "lyB"
                        wb = "wbA" if h == 0 else "wbB"
                        base_ = 2048 * t
                        t_bnd = None
                        if is_last_kb:
                            ecol = 20 + 2 * t + h
                            bc = base_ + 512 * (nj - 1)
                            t_bnd = S.add("vector", lambda e, lyb=lyb, bc=bc,
                                          ecol=ecol, t=t: e.scalar_tensor_tensor(
                                out=s[lyb][:, bc:bc + 512],
                                in0=s[lyb][:, bc:bc + 512],
                                scalar=s["aggv"][:, ecol:ecol + 1],
                                in1=s["m01"][:, 512 * t:512 * (t + 1)],
                                op0=ALU.add, op1=ALU.mult),
                                wait=(abs_tok[(h, t)],), inc="dv")
                        toks = []
                        if nmain > 0:
                            bcol = 16 + 2 * t + h
                            scol_ = 4 * kb_i + 2 * t + h
                            t_l1 = S.add("scalar", lambda e, lyb=lyb, wb=wb,
                                         base_=base_, nmain=nmain, bcol=bcol,
                                         scol_=scol_: e.activation(
                                out=s[wb][:, base_:base_ + 512 * nmain],
                                in_=s[lyb][:, base_:base_ + 512 * nmain],
                                func=AF.Ln, bias=s["aggv"][:, bcol:bcol + 1],
                                accum_out=s["aggv"][:, scol_:scol_ + 1]),
                                wait=(abs_tok[(h, t)], wb_rel.get(h)), inc="sc")
                            toks.append(t_l1)
                        if is_last_kb:
                            bcm = base_ + 512 * (nj - 1)
                            scol2 = 8 + 2 * t + h
                            t_l2 = S.add("scalar", lambda e, lyb=lyb, wb=wb, bcm=bcm,
                                         scol2=scol2: e.activation(
                                out=s[wb][:, bcm:bcm + 512],
                                in_=s[lyb][:, bcm:bcm + 512],
                                func=AF.Ln, bias=1.0,
                                accum_out=s["aggv"][:, scol2:scol2 + 1]),
                                wait=(t_bnd, wb_rel.get(h), abs_tok[(h, t)]),
                                inc="sc")
                            toks.append(t_l2)
                        ln_tok[(h, t)] = toks[-1]
                if halo:
                    t_hm = S.add("vector", lambda e, nj=nj, j0=j0:
                                 e.scalar_tensor_tensor(
                        out=s["hA"][:, 0:512 * nj], in0=s["hA"][:, 0:512 * nj],
                        scalar=s["aggv"][:, 28:29],
                        in1=s["h01"][:, 512 * j0:512 * (j0 + nj)],
                        op0=ALU.add, op1=ALU.mult), wait=(habs_tok,), inc="dv")
                    scol3 = 29 + kb_i
                    hln_tok = S.add("scalar", lambda e, nj=nj, scol3=scol3:
                                    e.activation(
                        out=s["whalo"][:, 0:512 * nj], in_=s["hA"][:, 0:512 * nj],
                        func=AF.Ln, bias=1.0,
                        accum_out=s["aggv"][:, scol3:scol3 + 1]),
                        wait=(t_hm, wh_rel), inc="sc")
                    if st.get("taps") and hp == 0 and kb_i == len(kblocks) - 1:
                        S.dma("gpsimd", _ap(d["dbg_hA"]), s["hA"][:, 0:2048],
                              wait=(t_hm,), sem="dbg")
                        S.dma("gpsimd", _ap(d["dbg_whalo"]), s["whalo"][:, 0:2048],
                              wait=(hln_tok,), sem="dbg")
                for jj in range(nj):
                    j = j0 + jj
                    alt = j % 2
                    t_v = S.dma("sync", s[f"vb{alt}"][:],
                                _ap(d["v_scr"])[hp][:, 512 * j:512 * (j + 1)],
                                wait=(vb_rel[alt], kvtok), sem=f"vb{alt}")
                    for i in range(4):
                        kt = 4 * jj + i
                        palt = kt % 2
                        for t in (0, 1):
                            tp2 = None
                            for h in (0, 1):
                                wb = "wbA" if h == 0 else "wbB"
                                tp2 = S.add("tensor", lambda e, wb=wb, t=t, h=h,
                                            kt=kt, palt=palt: e.transpose(
                                    p[f"pw{palt}"][:, 128 * h:128 * (h + 1)],
                                    s[wb][:, 2048 * t + 128 * kt:
                                           2048 * t + 128 * (kt + 1)],
                                    s["identb"][:]),
                                    wait=(ln_tok[(h, t)],)
                                    + ((pw_rel[palt],) if h == 0 else ()),
                                    inc="pe" if h == 1 else None)
                            wb_rel[0] = tp2
                            wb_rel[1] = tp2
                            t_ev = S.add("vector", lambda e, palt=palt: e.tensor_copy(
                                out=s[f"wt{palt}"][:], in_=p[f"pw{palt}"][:, 0:256]),
                                wait=(tp2, wt_rel[palt]), inc="dv")
                            pw_rel[palt] = t_ev
                            av2 = None
                            for h in (0, 1):
                                av2 = S.add("tensor", lambda e, h=h, t=t, i=i,
                                            alt=alt, palt=palt,
                                            st_=(not av_started): e.matmul(
                                    p["pav"][64 * h:64 * (h + 1),
                                             128 * t:128 * (t + 1)],
                                    s[f"vb{alt}"][:, 128 * i + 64 * h:
                                                  128 * i + 64 * (h + 1)],
                                    s[f"wt{palt}"][:, 128 * h:128 * (h + 1)],
                                    start=st_, stop=False, skip_group_check=True,
                                    tile_position=(0, 64 * h)),
                                    wait=(t_ev, t_v)
                                    + ((bank_rel["pav"],) if not av_started else ()),
                                    inc="pe" if h == 1 else None)
                            av_started = True
                            wt_rel[palt] = av2
                            vb_rel[alt] = av2
                    if halo:
                        for i in range(4):
                            kt = 4 * jj + i
                            t_t = S.add("tensor", lambda e, kt=kt: e.transpose(
                                p["pw0"][:, 256:384],
                                s["whalo"][:, 128 * kt:128 * (kt + 1)],
                                s["identb"][:]),
                                wait=(hln_tok, wth_rel, pw_rel[0]), inc="pe")
                            wh_rel = t_t
                            t_ev = S.add("vector", lambda e: e.tensor_copy(
                                out=s["wth"][:], in_=p["pw0"][:, 256:384]),
                                wait=(t_t,), inc="dv")
                            pw_rel[0] = t_ev
                            av2 = None
                            for h in (0, 1):
                                av2 = S.add("tensor", lambda e, h=h, i=i, alt=alt:
                                            e.matmul(
                                    p["pav"][64 * h:64 * (h + 1), 256:272],
                                    s[f"vb{alt}"][:, 128 * i + 64 * h:
                                                  128 * i + 64 * (h + 1)],
                                    s["wth"][:, 64 * h:64 * h + 16],
                                    start=False, stop=False, skip_group_check=True,
                                    tile_position=(0, 64 * h)),
                                    wait=(t_ev,), inc="pe" if h == 1 else None)
                            wth_rel = av2
                            vb_rel[alt] = av2
            w_ln = tuple(ln_tok[(h, t)] for t in (0, 1) for h in (0, 1))
            cols = sorted(set(sw_cols))
            base_c = cols[0]
            first = True
            for c2 in cols[1:]:
                S.add("vector", lambda e, base_c=base_c, c2=c2: e.tensor_add(
                    s["aggv"][:, base_c:base_c + 4],
                    s["aggv"][:, base_c:base_c + 4],
                    s["aggv"][:, c2:c2 + 4]), wait=w_ln if first else ())
                first = False
            S.add("vector", lambda e, base_c=base_c: e.tensor_scalar(
                out=s["aggv"][:, base_c:base_c + 4],
                in0=s["aggv"][:, base_c:base_c + 4],
                scalar1=1e-6, scalar2=None, op0=ALU.add),
                wait=w_ln if first else ())
            t_sw1 = S.add("vector", lambda e: e.nop(), inc="dv")
            S.add("scalar", lambda e, base_c=base_c: e.activation(
                out=s["aggv"][:, 12:16], in_=s["aggv"][:, base_c:base_c + 4],
                func=AF.Ln), wait=(t_sw1,))
            t_rsw = S.add("scalar", lambda e: e.activation(
                out=s["aggv"][:, 12:16], in_=s["aggv"][:, 12:16], func=AF.Exp,
                scale=-1.0), inc="sc")
            t_rswh = None
            if halo:
                S.add("vector", lambda e: e.scalar_tensor_tensor(
                    out=s["aggv"][:, 29:30], in0=s["aggv"][:, 29:30], scalar=1e-6,
                    in1=s["aggv"][:, 30:31], op0=ALU.add, op1=ALU.add),
                    wait=(hln_tok,))
                t_sw2 = S.add("vector", lambda e: e.nop(), inc="dv")
                S.add("scalar", lambda e: e.activation(
                    out=s["aggv"][:, 26:27], in_=s["aggv"][:, 29:30], func=AF.Ln),
                    wait=(t_sw2,))
                t_rswh = S.add("scalar", lambda e: e.activation(
                    out=s["aggv"][:, 26:27], in_=s["aggv"][:, 26:27], func=AF.Exp,
                    scale=-1.0), inc="sc")
            S.add("tensor", lambda e: e.transpose(
                p["pm"][0:2, 0:128], s["aggv"][:, 12:14], s["identf"][:]),
                wait=(t_rsw, bank_rel["pm"]))
            t_tr = S.add("tensor", lambda e: e.transpose(
                p["pm"][0:2, 128:256], s["aggv"][:, 14:16], s["identf"][:]),
                inc="pe")
            S.add("vector", lambda e: e.tensor_copy(
                out=s["rswT"][0:2, 0:256], in_=p["pm"][0:2, 0:256]), wait=(t_tr,))
            t_e2 = S.add("vector", lambda e: e.nop(), inc="dv")
            if halo:
                t_trh = S.add("tensor", lambda e: e.transpose(
                    p["pm"][0:1, 256:384], s["aggv"][:, 26:27], s["identf"][:]),
                    wait=(t_rswh, t_e2), inc="pe")
                S.add("vector", lambda e: e.tensor_copy(
                    out=s["rswT"][0:1, 256:272], in_=p["pm"][0:1, 256:272]),
                    wait=(t_trh,))
                t_e2 = S.add("vector", lambda e: e.tensor_copy(
                    out=s["rswT"][0:1, 272:288], in_=p["pm"][0:1, 320:336]),
                    inc="dv")
            S.add("tensor", lambda e: e.matmul(
                p["pm"][:, 0:128], s["sel2"][0:2, :], s["rswT"][0:2, 0:128],
                start=True, stop=True), wait=(t_e2,))
            t_b1 = S.add("tensor", lambda e: e.matmul(
                p["pm"][:, 128:256], s["sel2"][0:2, :], s["rswT"][0:2, 128:256],
                start=False, stop=True, skip_group_check=True),
                inc=None if halo else "pe")
            if halo:
                S.add("tensor", lambda e: e.matmul(
                    p["pm"][0:64, 256:272], s["ones128"][0:32, 0:64],
                    s["rswT"][0:32, 256:272], start=False, stop=True,
                    skip_group_check=True))
                t_b1 = S.add("tensor", lambda e: e.matmul(
                    p["pm"][64:128, 256:272], s["ones128"][0:32, 0:64],
                    s["rswT"][0:32, 272:288], start=False, stop=True,
                    tile_position=(0, 64), skip_group_check=True), inc="pe")
            nw = 272 if halo else 256
            t_bc = S.add("vector", lambda e, nw=nw: e.tensor_copy(
                out=s["bcast"][:, 0:nw], in_=p["pm"][:, 0:nw]), wait=(t_b1,),
                inc="dv")
            bank_rel["pm"] = t_bc
            w_avs = (wt_rel[0], wt_rel[1], wth_rel if halo else None)
            t_on = S.add("vector", lambda e, hp=hp, nw=nw: e.tensor_mul(
                s[f"outn{hp}"][:, 0:nw], p["pav"][:, 0:nw], s["bcast"][:, 0:nw]),
                wait=w_avs + (t_bc, outn_rel[hp]), inc="dv")
            bank_rel["pav"] = t_on
            if st.get("taps") and halo and hp == 0:
                S.dma("gpsimd", _ap(d["dbg_aggv"]), s["aggv"][:], wait=(t_on,),
                      sem="dbg")
                S.dma("gpsimd", _ap(d["dbg_rswT"]), s["rswT"][:], wait=(t_on,),
                      sem="dbg")
                S.dma("gpsimd", _ap(d["dbg_bcast"]), s["bcast"][:], wait=(t_on,),
                      sem="dbg")
                tapo = S.dma("gpsimd", _ap(d["dbg_outn"]), s["outn0"][:],
                             wait=(t_on,), sem="dbg")
                outn_rel[hp] = tapo
        nw = 272 if halo else 256
        t_mm = None
        for f in range(NCH):
            for hp in range(NCH):
                t_mm = S.add("tensor", lambda e, f=f, hp=hp, nw=nw: e.matmul(
                    p["pav"][:, 0:nw], s[f"wproj{hp}"][:, 128 * f:128 * (f + 1)],
                    s[f"outn{hp}"][:, 0:nw], start=(hp == 0), stop=(hp == NCH - 1)),
                    wait=((bank_rel["pav"], st["wproj_ready"]) if hp == 0 else ()),
                    inc="pe" if hp == NCH - 1 else None)
            t_r1 = S.dma("sync", s["resbuf"][:, 0:256],
                         _ap(d["xT"])[128 * f:128 * (f + 1), 512 * g:512 * g + 256],
                         wait=(resbuf_rel,), sem="res")
            wres = [t_r1]
            if halo:
                t_r2 = S.dma("sync", s["resbuf"][:, 256:272],
                             _ap(d["xT"])[128 * f:128 * (f + 1), T:T + NHALO],
                             sem="res")
                wres.append(t_r2)
            t_x2 = S.add("vector", lambda e, nw=nw: e.tensor_add(
                s["x2st"][:, 0:nw], p["pav"][:, 0:nw], s["resbuf"][:, 0:nw]),
                wait=tuple(wres) + (t_mm, x2st_rel), inc="dv")
            resbuf_rel = t_x2
            bank_rel["pav"] = t_x2
            t_s1 = S.dma("gpsimd", _ap(d["x2_scr"])[128 * f:128 * (f + 1),
                                                    256 * g:256 * (g + 1)],
                         s["x2st"][:, 0:256], wait=(t_x2,), sem="x2w")
            x2st_rel = t_s1
            if halo:
                t_s2 = S.dma("gpsimd",
                             _ap(d["x2_scr"])[128 * f:128 * (f + 1), NOWN:NTOK],
                             s["x2st"][:, 256:272], wait=(t_s1,), sem="x2w")
                x2st_rel = t_s2
        for hp in range(NCH):
            outn_rel[hp] = t_mm
        x2_last = x2st_rel
    st["x2_done"] = x2_last


def _phase_c(S, d, s, p, st):
    x2tok = st["x2_done"]
    terntok = st["tern_done"]
    WB = [512, 512, 512, 512, NHALO]
    xa_rel = [None] * NCH
    x2ln_rel = [None] * NCH
    bank_rel = {nm: None for nm in
                ("pA0", "pA1", "pB0", "pB1", "pav", "pm", "pc6", "pc7")}
    pm_rel = None
    mx_rel = None
    sb_rel = None
    kx_done = []
    sig_done = []
    stats_rel = [None, None]
    st["_sq_rel"] = None
    S.add("vector", lambda e: e.memset(s["zrow"][:], 0.0), wait=(x2tok,))
    if st.get("ctaps"):
        S.add("vector", lambda e: e.memset(s["sigb"][:, 0:2080], 0.0))
        S.add("vector", lambda e: e.memset(s["hsb"][:, 0:2080], 0.0))

    for tb in range(5):
        width = WB[tb]
        col0 = 512 * tb
        ch_tok = []
        for c in range(NCH):
            t_l = S.dma("sync", s[f"xa{c}"][:, 0:width],
                        _ap(d["x2_scr"])[128 * c:128 * (c + 1), col0:col0 + width],
                        wait=(xa_rel[c], x2tok), sem=f"xa{c}")
            ch_tok.append(t_l)
        t_re = _ln_block(S, s, p, st, ch_tok, width, stats_rel, sq_name="mx")
        x2ln_tok = []
        for c in range(NCH):
            S.add("vector", lambda e, c=c, width=width: e.tensor_sub(
                s["mx"][:, 0:width], s[f"xa{c}"][:, 0:width], s["mb"][:, 0:width]),
                wait=(t_re, ch_tok[c], st["_sq_rel"] if c == 0 else None))
            t_x = S.add("vector", lambda e, c=c, width=width: e.tensor_mul(
                s[f"x2ln{c}"][:, 0:width], s["mx"][:, 0:width], s["rb"][:, 0:width]),
                wait=(x2ln_rel[c],), inc="dv")
            x2ln_tok.append(t_x)
            xa_rel[c] = t_x
        st["_sq_rel"] = x2ln_tok[-1]
        t_mx = S.add("vector", lambda e, width=width: e.scalar_tensor_tensor(
            out=s["mx"][:, 0:width], in0=s["x2ln0"][:, 0:width], scalar=-1.0,
            in1=s["x2ln0"][:, 0:width], op0=ALU.mult, op1=ALU.max),
            wait=(x2ln_tok[0], mx_rel), inc="dv")
        for c in range(1, NCH):
            S.add("vector", lambda e, c=c, width=width: e.scalar_tensor_tensor(
                out=s["mxt"][:, 0:width], in0=s[f"x2ln{c}"][:, 0:width], scalar=-1.0,
                in1=s[f"x2ln{c}"][:, 0:width], op0=ALU.mult, op1=ALU.max),
                wait=(x2ln_tok[c],))
            t_mx = S.add("vector", lambda e, width=width: e.tensor_max(
                s["mx"][:, 0:width], s["mx"][:, 0:width], s["mxt"][:, 0:width]),
                inc="dv")
        ntile = (width + 127) // 128
        red_toks = []
        for i in range(ntile):
            tw = min(128, width - 128 * i)
            t_t = S.add("tensor", lambda e, i=i, tw=tw: e.transpose(
                p["pm"][0:tw, 0:128], s["mx"][:, 128 * i:128 * i + tw],
                s["identf"][:]), wait=(t_mx, pm_rel), inc="pe")
            t_ev = S.add("vector", lambda e, tw=tw: e.tensor_copy(
                out=s["mxt"][0:tw, 0:128], in_=p["pm"][0:tw, 0:128]),
                wait=(t_t,), inc="dv")
            pm_rel = t_ev
            t_rd = S.add("vector", lambda e, tw=tw: e.reduce_max(
                out=s["mxt"][0:tw, 128:129], in_=s["mxt"][0:tw, 0:128], axis=AX.X),
                inc="dv")
            t_t2 = S.add("tensor", lambda e, i=i, tw=tw: e.transpose(
                p["pav"][0:1, 128 * i:128 * i + tw], s["mxt"][0:tw, 128:129],
                s["identf"][0:tw, 0:tw]),
                wait=(t_rd,) + ((bank_rel["pav"],) if i == 0 else ()), inc="pe")
            red_toks.append(t_t2)
            pm_rel = t_t2
        mx_rel = red_toks[-1]
        t_z = S.add("vector", lambda e, width=width: e.tensor_copy(
            out=s["zrow"][0:1, 0:width], in_=p["pav"][0:1, 0:width]),
            wait=(red_toks[-1],), inc="dv")
        bank_rel["pav"] = t_z
        t_bs = S.add("tensor", lambda e, width=width: e.matmul(
            p["pm"][:, 0:width], s["ones128"][:], s["zrow"][:, 0:width],
            start=True, stop=True), wait=(t_z,), inc="pe")
        t_se = S.add("vector", lambda e, width=width: e.tensor_scalar(
            out=s["sb_"][:, 0:width], in0=p["pm"][:, 0:width],
            scalar1=1e-5, scalar2=None, op0=ALU.max),
            wait=(t_bs, sb_rel), inc="dv")
        pm_rel = t_se
        S.add("scalar", lambda e, width=width: e.activation(
            out=s["lns"][:, 0:width], in_=s["sb_"][:, 0:width], func=AF.Ln),
            wait=(t_se,))
        t_se = S.add("scalar", lambda e, width=width: e.activation(
            out=s["sb_"][:, 0:width], in_=s["lns"][:, 0:width], func=AF.Exp,
            scale=-1.0, bias=s["ln127"][:]), inc="sc")
        S.add("vector", lambda e, width=width, col0=col0: e.tensor_scalar(
            out=s["sigb"][:, col0:col0 + width], in0=s["sb_"][:, 0:width],
            scalar1=s["sw1col"][:], scalar2=None, op0=ALU.mult),
            wait=(t_se, terntok))
        t_gp = S.add("vector", lambda e: e.nop(), inc="dv")
        S.add("scalar", lambda e, width=width, col0=col0: e.activation(
            out=s["sigb"][:, col0:col0 + width],
            in_=s["sigb"][:, col0:col0 + width], func=AF.Ln), wait=(t_gp,))
        t_ge = S.add("scalar", lambda e, width=width, col0=col0: e.activation(
            out=s["sigb"][:, col0:col0 + width],
            in_=s["sigb"][:, col0:col0 + width], func=AF.Exp, scale=-1.0),
            inc="sc")
        sig_done.append(t_ge)
        for c in range(NCH):
            S.add("vector", lambda e, c=c, width=width: e.tensor_mul(
                s["i32s"][:, 0:width], s[f"x2ln{c}"][:, 0:width],
                s["sb_"][:, 0:width]), wait=(t_se, x2ln_tok[c]))
            t_k = S.add("vector", lambda e, c=c, width=width, col0=col0:
                        e.tensor_copy(
                out=s[f"Q{c}"][:, col0:col0 + width], in_=s["i32s"][:, 0:width]),
                inc="dv")
            x2ln_rel[c] = t_k
            kx_done.append(t_k)
        sb_rel = kx_done[-1]
    kx_tok = kx_done[-1]
    sig_tok = sig_done[-1]
    if st.get("ctaps"):
        S.dma("gpsimd", _ap(d["dbg_kx"]), s["Q0"][:, 0:NTOK], wait=(kx_tok,),
              sem="dbg")
        S.dma("gpsimd", _ap(d["dbg_sigb"]), s["sigb"][:, 0:2080],
              wait=(sig_tok,), sem="dbg")

    t1t_rel = None
    hsb_rel = None
    h2_rel = None
    m2_tok = None
    hout_toks = []
    for hc in range(HCH):
        t1_toks = []
        for dc in range(NCH):
            t_l = S.dma("sync", s["t1t"][:, 128 * dc:128 * (dc + 1)],
                        _ap(d["t1_scr"])[dc][:, 128 * hc:128 * (hc + 1)],
                        wait=(t1t_rel, terntok), sem=f"t1_{dc}")
            t1_toks.append(t_l)
        ev_toks = []
        t_mm = None
        for tb in range(5):
            width = WB[tb]
            col0 = 512 * tb
            alt = tb % 2
            bank = "pA0" if alt == 0 else "pA1"
            for dc in range(NCH):
                t_mm = S.add("tensor", lambda e, dc=dc, bank=bank, width=width,
                             col0=col0: e.matmul(
                    p[bank][:, 0:width], s["t1t"][:, 128 * dc:128 * (dc + 1)],
                    s[f"Q{dc}"][:, col0:col0 + width],
                    start=(dc == 0), stop=(dc == NCH - 1)),
                    wait=(t1_toks[dc], kx_tok)
                    + ((bank_rel[bank],) if dc == 0 else ()),
                    inc="pe" if dc == NCH - 1 else None)
            if tb < 4:
                t_ev = S.add("vector", lambda e, bank=bank, tb=tb, col0=col0:
                             e.tensor_mul(
                    s["hsb"].ap_at(260 * 2 * tb + 1, [[260, 2], [1, 256]]),
                    p[bank][:, 0:512], s["sigb"][:, col0:col0 + 512]),
                    wait=(t_mm, sig_tok, hsb_rel), inc="dv")
            else:
                S.add("vector", lambda e, bank=bank: e.tensor_mul(
                    s["hacc"][:, 0:NHALO], p[bank][:, 0:NHALO],
                    s["sigb"][:, NOWN:NTOK]), wait=(t_mm, sig_tok, m2_tok))
                S.add("vector", lambda e: e.tensor_mul(
                    s["hacc"][:, 0:NHALO], s["hacc"][:, 0:NHALO], s["hvalid"][:]))
                S.add("vector", lambda e: e.tensor_copy(
                    out=s["hsb"].ap_at(0, [[260, 8]]), in_=s["hacc"][:, 0:8]))
                t_ev = S.add("vector", lambda e: e.tensor_copy(
                    out=s["hsb"].ap_at(257, [[260, 8]]), in_=s["hacc"][:, 8:16]),
                    inc="dv")
            bank_rel[bank] = t_ev
            ev_toks.append(t_ev)
        t1t_rel = t_mm
        S.add("vector", lambda e, hc=hc: e.tensor_scalar(
            out=s["hacc"][:, 0:2048],
            in0=s["hsb"].ap_at(0, [[260, 8], [1, 256]]),
            scalar1=s["convw"][:, 3 * hc:3 * hc + 1], scalar2=None, op0=ALU.mult),
            wait=tuple(ev_toks) + (m2_tok,))
        S.add("vector", lambda e, hc=hc: e.scalar_tensor_tensor(
            out=s["hacc2"][:, 0:2048],
            in0=s["hsb"].ap_at(1, [[260, 8], [1, 256]]),
            scalar=s["convw"][:, 3 * hc + 1:3 * hc + 2],
            in1=s["hacc"][:, 0:2048], op0=ALU.mult, op1=ALU.add))
        t_cv = S.add("vector", lambda e, hc=hc: e.scalar_tensor_tensor(
            out=s["hacc"][:, 0:2048],
            in0=s["hsb"].ap_at(2, [[260, 8], [1, 256]]),
            scalar=s["convw"][:, 3 * hc + 2:3 * hc + 3],
            in1=s["hacc2"][:, 0:2048], op0=ALU.mult, op1=ALU.add), inc="dv")
        hsb_rel = t_cv
        if st.get("ctaps") and hc == HCH - 1:
            S.dma("gpsimd", _ap(d["dbg_hsb"]), s["hsb"][:, 0:2080],
                  wait=tuple(ev_toks), sem="dbg")
        t_si = S.add("scalar", lambda e, hc=hc: e.activation(
            out=s["h2outv"][:, 0:2048], in_=s["hacc"][:, 0:2048], func=AF.Silu,
            bias=s["convb"][:, hc:hc + 1]), wait=(t_cv, h2_rel), inc="sc")
        if hc == 0:
            t_m2 = S.add("vector", lambda e: e.scalar_tensor_tensor(
                out=s["m2glob"][:, 0:2048], in0=s["h2outv"][:, 0:2048], scalar=-1.0,
                in1=s["h2outv"][:, 0:2048], op0=ALU.mult, op1=ALU.max),
                wait=(t_si,), inc="dv")
        else:
            S.add("vector", lambda e: e.scalar_tensor_tensor(
                out=s["hacc2"][:, 0:2048], in0=s["h2outv"][:, 0:2048], scalar=-1.0,
                in1=s["h2outv"][:, 0:2048], op0=ALU.mult, op1=ALU.max),
                wait=(t_si,))
            t_m2 = S.add("vector", lambda e: e.tensor_max(
                s["m2glob"][:, 0:2048], s["m2glob"][:, 0:2048],
                s["hacc2"][:, 0:2048]), inc="dv")
        m2_tok = t_m2
        t_st = S.dma("gpsimd", _ap(d["h2_scr"])[hc], s["h2outv"][:, 0:2048],
                     wait=(t_si,), sem="h2w")
        h2_rel = t_st
        hout_toks.append(t_st)

    if st.get("ctaps"):
        S.dma("gpsimd", _ap(d["dbg_m2"]), s["m2glob"][:, 0:2048],
              wait=(m2_tok,), sem="dbg")
    t_sr = None
    for q in range(4):
        toks = []
        for i in range(4):
            ii = 4 * q + i
            t_t = S.add("tensor", lambda e, ii=ii: e.transpose(
                p["pav"][:, 0:128], s["m2glob"][:, 128 * ii:128 * (ii + 1)],
                s["identf"][:]), wait=(m2_tok, bank_rel["pav"]), inc="pe")
            t_ev = S.add("vector", lambda e: e.tensor_copy(
                out=s["mxt"][:, 0:128], in_=p["pav"][:, 0:128]),
                wait=(t_t,), inc="dv")
            bank_rel["pav"] = t_ev
            t_rd = S.add("vector", lambda e: e.reduce_max(
                out=s["mxt"][:, 128:129], in_=s["mxt"][:, 0:128], axis=AX.X),
                inc="dv")
            t_t2 = S.add("tensor", lambda e, i=i: e.transpose(
                p["pc6"][0:1, 128 * i:128 * (i + 1)], s["mxt"][:, 128:129],
                s["identf"][:]),
                wait=(t_rd,) + ((bank_rel["pc6"],) if i == 0 else ()), inc="pe")
            toks.append(t_t2)
        t_z = S.add("vector", lambda e: e.tensor_copy(
            out=s["zrow"][0:1, 0:512], in_=p["pc6"][0:1, 0:512]),
            wait=tuple(toks), inc="dv")
        t_bs = S.add("tensor", lambda e: e.matmul(
            p["pc6"][:, 0:512], s["ones128"][:], s["zrow"][:, 0:512],
            start=True, stop=True), wait=(t_z,), inc="pe")
        t_sr = S.add("vector", lambda e, q=q: e.tensor_scalar(
            out=s["s2row"][:, 512 * q:512 * (q + 1)], in0=p["pc6"][:, 0:512],
            scalar1=1e-5, scalar2=None, op0=ALU.max), wait=(t_bs,), inc="dv")
        bank_rel["pc6"] = t_sr
    S.add("scalar", lambda e: e.activation(
        out=s["s2row"][:, 0:NOWN], in_=s["s2row"][:, 0:NOWN], func=AF.Ln),
        wait=(t_sr,))
    t_sg2 = S.add("scalar", lambda e: e.activation(
        out=s["s2row"][:, 0:NOWN], in_=s["s2row"][:, 0:NOWN], func=AF.Exp,
        scale=-1.0, bias=s["ln127"][:]), inc="sc")
    if st.get("ctaps"):
        S.dma("gpsimd", _ap(d["dbg_s2"]), s["s2row"][:, 0:2048],
              wait=(t_sg2,), sem="dbg")

    W2BANKS = ["pA0", "pA1", "pB0", "pB1", "pav", "pm", "pc6", "pc7"]
    t2t_rel = [None, None]
    h2in_rel = [None, None]
    k2_rel = None
    ost_rel = [None, None]
    res_rel = [hout_toks[-1]] * NCH
    y_toks = []
    pm_rel2 = pm_rel
    for tb in range(4):
        col0 = 512 * tb
        t_gq = S.add("vector", lambda e, col0=col0: e.tensor_scalar(
            out=s["sg2b"][:], in0=s["s2row"][:, col0:col0 + 512],
            scalar1=s["sw2col"][:], scalar2=None, op0=ALU.mult),
            wait=(t_sg2, terntok), inc="dv")
        S.add("scalar", lambda e: e.activation(
            out=s["sg2b"][:], in_=s["sg2b"][:], func=AF.Ln), wait=(t_gq,))
        t_ge = S.add("scalar", lambda e: e.activation(
            out=s["sg2b"][:], in_=s["sg2b"][:], func=AF.Exp, scale=-1.0),
            inc="sc")
        mm_last = None
        for hc in range(HCH):
            alt = hc % 2
            t_lh = S.dma("sync", s[f"h2in{alt}"][:],
                         _ap(d["h2_scr"])[hc][:, col0:col0 + 512],
                         wait=(h2in_rel[alt], hout_toks[hc]), sem=f"h2{alt}")
            t_lt = S.dma("sync", s[f"t2t{alt}"][:, 0:D], _ap(d["t2_scr"])[hc],
                         wait=(t2t_rel[alt], terntok), sem=f"t2{alt}")
            S.add("vector", lambda e, alt=alt, col0=col0: e.tensor_mul(
                s["i32s"][:, 0:512], s[f"h2in{alt}"][:],
                s["s2row"][:, col0:col0 + 512]), wait=(t_lh, t_ge, k2_rel))
            t_k2 = S.add("vector", lambda e: e.tensor_copy(
                out=s["k2b"][:], in_=s["i32s"][:, 0:512]), inc="dv")
            h2in_rel[alt] = t_k2
            for fc in range(NCH):
                bank = W2BANKS[fc]
                mm_last = S.add("tensor", lambda e, fc=fc, alt=alt, bank=bank,
                                hc=hc: e.matmul(
                    p[bank][:, 0:512], s[f"t2t{alt}"][:, 128 * fc:128 * (fc + 1)],
                    s["k2b"][:], start=(hc == 0), stop=(hc == HCH - 1)),
                    wait=(t_lt, t_k2) + ((bank_rel[bank],) if hc == 0 else ()),
                    inc="pe" if fc == NCH - 1 else None)
            k2_rel = mm_last
            t2t_rel[alt] = mm_last
        for fc in range(NCH):
            bank = W2BANKS[fc]
            alt = fc % 2
            t_rl = S.dma("sync", s[f"xa{fc}"][:, 0:512],
                         _ap(d["x2_scr"])[128 * fc:128 * (fc + 1), col0:col0 + 512],
                         wait=(res_rel[fc],), sem=f"xa{fc}")
            S.add("vector", lambda e, bank=bank, alt=alt: e.tensor_mul(
                s[f"ostage{alt}"][:, 0:512], p[bank][:, 0:512], s["sg2b"][:]),
                wait=(mm_last, ost_rel[alt]))
            t_o = S.add("vector", lambda e, fc=fc, alt=alt: e.tensor_add(
                s[f"ostage{alt}"][:, 0:512], s[f"ostage{alt}"][:, 0:512],
                s[f"xa{fc}"][:, 0:512]), wait=(t_rl,), inc="dv")
            bank_rel[bank] = t_o
            t_y = S.dma("gpsimd", _ap(d["y"])[128 * fc:128 * (fc + 1),
                                              col0:col0 + 512],
                        s[f"ostage{alt}"][:, 0:512], wait=(t_o,), sem=f"yw{alt}")
            ost_rel[alt] = t_y
            res_rel[fc] = t_y
            y_toks.append(t_y)
    S.add("gpsimd", lambda e: e.nop(), wait=(y_toks[-2], y_toks[-1]))


_NC_CACHE = {}
DEBUG = []


def _get_nc(debug=()):
    key = tuple(debug)
    if key not in _NC_CACHE:
        _NC_CACHE[key] = build_program(debug=key)
    return _NC_CACHE[key]


def _host_inputs(inputs, core):
    import ml_dtypes
    bf16 = ml_dtypes.bfloat16
    b, par = core // 2, core % 2
    x = np.asarray(inputs["L0_attn_x"])[b]
    perm = np.empty(T, dtype=np.int64)
    for g in range(G):
        own = 512 * g + 256 * par
        oth = 512 * g + 256 * (1 - par)
        perm[512 * g:512 * g + 256] = np.arange(own, own + 256)
        perm[512 * g + 256:512 * (g + 1)] = np.arange(oth, oth + 256)
    halo_idx = np.zeros(NHALO, dtype=np.int64)
    halo_valid = np.zeros(NHALO, dtype=np.float32)
    halo_tau = np.full(NHALO, -1, dtype=np.int64)
    for m in range(G):
        a = 512 * m + 256 * par
        l, r = a - 1, a + 256
        if l >= 0:
            halo_idx[m] = l
            halo_valid[m] = 1.0
            halo_tau[m] = l
        if r < T:
            halo_idx[8 + m] = r
            halo_valid[8 + m] = 1.0
            halo_tau[8 + m] = r
    xT = np.ascontiguousarray(
        np.concatenate([x[perm], x[halo_idx]], axis=0).T.astype(np.float32))

    q_loc = np.arange(128)
    col = np.arange(512)
    mclamp = np.empty((128, 1024), np.float32)
    m01 = np.empty((128, 1024), np.float32)
    for t in (0, 1):
        valid_own = col[None, :256] <= (128 * t + q_loc)[:, None]
        valid_oth = np.full((128, 256), par == 1)
        valid = np.concatenate([valid_own, valid_oth], axis=1)
        mclamp[:, 512 * t:512 * (t + 1)] = np.where(valid, 15.0, -16384.0)
        m01[:, 512 * t:512 * (t + 1)] = valid.astype(np.float32)

    hclamp = np.full((128, T), -16384.0, np.float32)
    h01 = np.zeros((128, T), np.float32)
    for blk in (0, 64):
        for sl in range(NHALO):
            row = blk + sl
            if halo_valid[sl]:
                valid = perm <= halo_tau[sl]
                hclamp[row] = np.where(valid, 15.0, -16384.0)
                h01[row] = valid.astype(np.float32)

    sel2 = np.zeros((66, 128), np.float32)
    for r0 in (0, 32, 64):
        sel2[r0, 0:64] = 1.0
        sel2[r0 + 1, 64:128] = 1.0

    return {
        "xT": xT,
        "wqkv_t": np.ascontiguousarray(np.asarray(inputs["L0_attn_w_qkv"]).T.astype(np.float32)),
        "wky_t": np.ascontiguousarray(np.asarray(inputs["L0_attn_w_ky"]).T.astype(np.float32)),
        "wproj_t": np.ascontiguousarray(np.asarray(inputs["L0_attn_w_proj"]).T.astype(np.float32)),
        "w1_t": np.ascontiguousarray(np.asarray(inputs["L0_ffn_w1"]).T.astype(np.float32)),
        "w2_t": np.ascontiguousarray(np.asarray(inputs["L0_ffn_w2"]).T.astype(np.float32)),
        "convw": np.ascontiguousarray(np.asarray(inputs["L0_ffn_conv_w"]).reshape(DH, 3).astype(np.float32)),
        "convb": np.ascontiguousarray(np.asarray(inputs["L0_ffn_conv_b"]).reshape(DH, 1).astype(np.float32)),
        "identf": np.eye(128, dtype=np.float32),
        "identb": np.eye(128, dtype=np.float32).astype(bf16),
        "ones128": np.ones((128, 128), np.float32),
        "onesp": np.ones((128, 1), np.float32),
        "sel2": sel2,
        "mclamp": mclamp.astype(bf16),
        "m01": m01.astype(bf16),
        "hclamp": hclamp.astype(bf16),
        "h01": h01.astype(bf16),
        "hvalid": np.broadcast_to(halo_valid[None, :], (128, NHALO)).copy(),
        "ln127": np.full((128, 1), np.log(127.0), np.float32),
    }


def kernel(**inputs):
    nc = _get_nc(debug=tuple(DEBUG))
    in_maps = [_host_inputs(inputs, c) for c in range(8)]
    res = run_bass_kernel_spmd(nc, in_maps, core_ids=list(range(8)))
    out = np.empty((B, T, D), dtype=np.float32)
    for c in range(8):
        b, par = c // 2, c % 2
        y = res.results[c]["y"]
        for g in range(G):
            out[b, 512 * g + 256 * par:512 * g + 256 * par + 256, :] = \
                y[:, 256 * g:256 * (g + 1)].T
    kernel._last_results = res
    return out
